# version 39
# baseline (speedup 1.0000x reference)
import numpy as np
import ml_dtypes
import concourse.bass as bass
import concourse.mybir as mybir
from concourse.bass_utils import run_bass_kernel_spmd
from contextlib import ExitStack

F32 = mybir.dt.float32
BF16 = mybir.dt.bfloat16
OP = mybir.AluOpType
AF = mybir.ActivationFunctionType
AX = mybir.AxisListType.X

D = 1024
H = 4096
N = 64
ST = 512
TT = 128
T = 64
DK = D // 128
HJ = H // 128


def build_nc(L, debug_taps=(), paranoid=False):
    NS = L // ST
    CC = L // T
    n_ttile = L // TT

    nc = bass.Bass("TRN2", target_bir_lowering=False)

    x = nc.dram_tensor("x", [L, D], F32, kind="ExternalInput").ap()
    w1 = nc.dram_tensor("w1", [D, H], F32, kind="ExternalInput").ap()
    w2 = nc.dram_tensor("w2", [H, D], F32, kind="ExternalInput").ap()
    bT = nc.dram_tensor("bT", [D, 128], F32, kind="ExternalInput").ap()
    cT = nc.dram_tensor("cT", [N + 1, D], F32, kind="ExternalInput").ap()
    aN = nc.dram_tensor("aN", [N, N], F32, kind="ExternalInput").ap()
    aT = nc.dram_tensor("aT", [N, N], F32, kind="ExternalInput").ap()
    i64 = nc.dram_tensor("i64", [N, N], F32, kind="ExternalInput").ap()
    d_rep = nc.dram_tensor("d_rep", [128, D], F32, kind="ExternalInput").ap()
    ldr = nc.dram_tensor("ldr", [128, D], F32, kind="ExternalInput").ap()
    b1t = nc.dram_tensor("b1t", [128, HJ], F32, kind="ExternalInput").ap()
    ones8k = nc.dram_tensor("ones8k", [1, L], BF16, kind="ExternalInput").ap()
    eye128 = nc.dram_tensor("eye128", [128, 128], BF16, kind="ExternalInput").ap()
    cst = nc.dram_tensor("cst", [128, 4], F32, kind="ExternalInput").ap()
    o = nc.dram_tensor("o", [L, D], F32, kind="ExternalOutput").ap()
    w2s = nc.dram_tensor("w2s", [H, D], BF16, kind="Internal").ap()
    xscr2 = nc.dram_tensor("xscr2", [L // ST * 128, DK * ST], BF16, kind="Internal").ap()

    with ExitStack() as ctx:
        tens = {}
        def sb(name, shape, dt=F32):
            h = ctx.enter_context(nc.sbuf_tensor(name, shape, dt))
            tens[name] = h
            return h
        def psd(name):
            return ctx.enter_context(nc.psum_tensor(name, [128, 512], F32))
        def sem(name):
            return ctx.enter_context(nc.semaphore(name))

        w1b = sb("w1b", [128, DK * H], BF16)
        xt = sb("xt", [128, 2 * (ST // TT) * D])
        xnb = sb("xnb", [128, 2 * D], BF16)
        xfm = sb("xfm", [128, 2 * DK * ST], BF16)
        bu = sb("bu", [N + 1, L], BF16)
        hb = sb("hb", [128, HJ * ST], BF16)
        outb = sb("outb", [128, 4 * D])
        w2b = sb("w2b", [128, 16 * 512], BF16)
        dsb = sb("dsb", [128, D])
        bns = sb("bns", [128, 48])
        ssq = sb("ssq", [128, 8])
        rstd = sb("rstd", [128, n_ttile])
        mrn = sb("mrn", [128, n_ttile])
        nmt = sb("nmt", [128, 8])
        btb = sb("btb", [128, DK * 128], BF16)
        ctb = sb("ctb", [N + 1, D], BF16)
        b1tb = sb("b1tb", [128, HJ])
        dtb = sb("dtb", [128, 1])
        cstb = sb("cstb", [128, 4])
        abB = sb("abB", [N, N], BF16); abTB = sb("abTB", [N, N], BF16)
        zb0 = sb("zb0", [N, N], BF16); zb1 = sb("zb1", [N, N], BF16)
        zbT0 = sb("zbT0", [N, N], BF16); zbT1 = sb("zbT1", [N, N], BF16)
        p64T = sb("p64T", [N, N], BF16)
        ab2Tb = sb("ab2Tb", [N, N], BF16); ab3Tb = sb("ab3Tb", [N, N], BF16)
        ab4Tb = sb("ab4Tb", [N, N], BF16)
        Sst = sb("Sst", [N, 2 * CC], BF16)
        i128b = sb("i128b", [128, 128], BF16)
        INIT = sb("INIT", [N, CC], BF16)

        P = [psd(f"p{i}") for i in range(6)]
        PT = [ctx.enter_context(nc.psum_tensor(f"pt{i}", [128, 1024], BF16))
              for i in range(2)]

        sIN = sem("sIN"); sSCR = sem("sSCR"); sTR = sem("sTR"); sX2 = sem("sX2")
        sW2 = sem("sW2"); sW = sem("sW"); sOUT = sem("sOUT")
        mmBU = sem("mmBU"); mmS = sem("mmS"); mmH = sem("mmH"); mmF = sem("mmF")
        w2c = sem("w2c"); vS = sem("vS"); vBU = sem("vBU"); vFIN = sem("vFIN")
        vEV = sem("vEV"); vXT = sem("vXT"); sqSC = sem("sqSC"); xnSC = sem("xnSC")
        geSC = sem("geSC"); xsSC = sem("xsSC")
        sW1 = sem("sW1"); sW2X = sem("sW2X")
        semV0 = sem("semV0"); semA0 = sem("semA0"); semT0 = sem("semT0")
        mmT = sem("mmT"); vXF = sem("vXF")
        sW0 = sem("sW0"); sWA = sem("sWA"); sIN2 = sem("sIN2")

        all_sems = [sIN, sSCR, sTR, sX2, sW2, sW, sOUT, mmBU, mmS, mmH, mmF,
                    w2c, vS, vBU, vFIN, vEV, vXT, sqSC, xnSC, geSC, xsSC,
                    sW1, sW2X, semV0, semA0, semT0, mmT, vXF, sW0, sWA, sIN2]
        sem_nums = [s.num for s in all_sems]
        lo, hi = min(sem_nums), max(sem_nums)
        assert hi - lo + 1 == len(sem_nums), "semaphore range not contiguous"
        with nc.Block() as blk:
            @blk.gpsimd
            def _(g):
                g.dma_reset(range(lo, hi + 1))
                g.sem_clear(range(lo, hi + 1))

        _dmacnt = {}
        def pdma(eng, sm, out_ap, in_ap, **kw):
            c = _dmacnt.get(sm.num, 0)
            if paranoid and c > 0:
                eng.wait_ge(sm, c)
            eng.dma_start(out_ap, in_ap, **kw).then_inc(sm, 16)
            _dmacnt[sm.num] = c + 16

        xt_r = xt.rearrange("p (r j d) -> p r j d", r=2, d=D)
        xfm_r = xfm.rearrange("p (r k t) -> p r k t", r=2, t=ST)
        xfm_r2 = xfm.rearrange("p (r k j t) -> p r k j t", r=2, k=DK, t=TT)
        xnb_r = xnb.rearrange("p (r d) -> p r d", r=2)
        hb_r = hb.rearrange("p (j t) -> p j t", t=ST)
        outb_r = outb.rearrange("p (q d) -> p q d", d=D)
        w2b_r = w2b.rearrange("p (s f) -> p s f", s=16)
        w1b_r = w1b.rearrange("p (k h) -> p k h", k=DK)
        btb_r = btb.rearrange("p (k n) -> p k n", k=DK)
        bu_r = bu[0:N, :].rearrange("n (c i) -> n i c", i=T)
        Sst_r = Sst.rearrange("n (r c) -> n r c", r=2)
        x_r = x.rearrange("(s j p) d -> p s j d", j=ST // TT, p=TT)
        w2s_r = w2s.rearrange("(j p) d -> p j d", p=128)

        mv = outb[0:128, D + 4 * N:D + 4 * N + 2 * n_ttile]
        aNs = outb[0:N, D:D + N]
        aTs = outb[0:N, D + N:D + 2 * N]
        i64s = outb[0:N, D + 2 * N:D + 3 * N]
        zf = outb[0:N, D + 3 * N:D + 4 * N]

        NSMALL = 16 * 6

        with nc.Block() as blk:
            @blk.gpsimd
            def _(g):
                pdma(g, sW0, outb[:, 0:D], ldr[:, :])
                pdma(g, sW0, cstb[:], cst[:, :])
                pdma(g, sWA, aTs, aT[:, :])
                pdma(g, sWA, i64s, i64[:, :])
                pdma(g, sWA, aNs, aN[:, :])
                for dst, src in ((dsb[:], d_rep[:, :]), (b1tb[:], b1t[:, :]),
                                 (bu[N:N + 1, :], ones8k[:, :]),
                                 (i128b[:], eye128[:, :]), (ctb[:], cT[:, :])):
                    pdma(g, sW, dst, src)
                pdma(g, sW, btb[:], bT.rearrange("(k p) n -> p k n", p=128))

            @blk.vector
            def _(v):
                v.wait_ge(semA0, 1)
                v.reduce_sum(dtb[:], outb[:, 0:D], axis=AX)
                v.drain()
                v.tensor_scalar(dtb[:], dtb[:], cstb[:, 0:1], None, OP.mult)
                v.drain()
                v.wait_ge(sWA, 16 * 3)
                v.scalar_tensor_tensor(zf, aTs, dtb[0:N, 0:1], i64s,
                                       OP.mult, OP.add).then_inc(semV0, 1)
                v.wait_ge(semA0, 2)
                v.scalar_tensor_tensor(zf, aNs, dtb[0:N, 0:1], i64s,
                                       OP.mult, OP.add).then_inc(semV0, 1)

            @blk.scalar
            def _(s):
                s.wait_ge(sW0, 16 * 2)
                s.activation(outb[:, 0:D], outb[:, 0:D], AF.Exp).then_inc(semA0, 1)
                s.wait_ge(semV0, 1)
                s.copy(abTB[:], zf).then_inc(semA0, 1)
                s.wait_ge(semV0, 2)
                s.copy(abB[:], zf).then_inc(semA0, 1)
                for k in range(9):
                    zT_dst = (zbT0, zbT1)[k % 2]
                    z_dst = (zb0, zb1)[k % 2]
                    s.wait_ge(semT0, 2 * (k + 1))
                    s.copy(zT_dst[:], P[0][0:N, 0:N])
                    if k == 0:
                        s.copy(ab2Tb[:], P[0][0:N, 0:N])
                    if k == 1:
                        s.copy(ab4Tb[:], P[0][0:N, 0:N])
                    if k == 5:
                        s.copy(p64T[:], P[0][0:N, 0:N])
                    s.copy(z_dst[:], P[1][0:N, 0:N]).then_inc(semA0, 1)
                s.wait_ge(semT0, 19)
                s.copy(ab3Tb[:], P[0][0:N, 0:N])

            @blk.sync
            def _(sy):
                for sp in range(2):
                    for j in range(4):
                        pdma(sy, (sIN, sIN2)[sp % 2], xt_r[:, sp, j, :],
                             x_r[:, sp, j, :])

            @blk.tensor
            def _(t):
                t.wait_ge(semA0, 3)
                cur_z, cur_zT = abB, abTB
                for k in range(9):
                    if k > 0:
                        t.wait_ge(semA0, 3 + k)
                    t.matmul(P[0][0:N, 0:N], cur_z[:], cur_zT[:],
                             start=True, stop=True).then_inc(semT0, 1)
                    t.matmul(P[1][0:N, 0:N], cur_zT[:], cur_z[:],
                             start=True, stop=True).then_inc(semT0, 1)
                    cur_z, cur_zT = (zb0, zb1)[k % 2], (zbT0, zbT1)[k % 2]
                t.wait_ge(semA0, 12)
                t.matmul(P[0][0:N, 0:N], abB[:], ab2Tb[:],
                         start=True, stop=True).then_inc(semT0, 1)
        P512T = zbT0

        with nc.Block() as blk:
            @blk.sync
            def _(sy):
                for s in range(2, NS):
                    for j in range(4):
                        sy.wait_ge(xnSC, 4 * (s - 2) + j + 1)
                        pdma(sy, (sIN, sIN2)[s % 2], xt_r[:, s % 2, j, :],
                             x_r[:, s, j, :])
                sy.wait_ge(xnSC, 4 * (NS - 1))
                pdma(sy, sX2, xt_r[:, 0, :, :], x_r[:, 0, :, :])
                sy.wait_ge(sSCR, 16 * (NS - 1))
                sy.wait_ge(mmBU, NS - 1)
                pdma(sy, sTR, xfm_r[:, 0, :, :], xscr2[0:128, :])

            mvr = mv.rearrange("p (j two) -> p two j", two=2)

            def tev(v, slot, j, ti_seen):
                v.wait_ge(mmT, ti_seen + 1)
                v.tensor_copy(xfm_r2[:, slot, :, j, :],
                              PT[ti_seen % 2][0:128, 0:1024]).then_inc(vXF, 1)

            @blk.vector
            def _(v):
                def stats(s):
                    v.drain()
                    v.wait_ge((sIN, sIN2)[s % 2], 16 * 4 * (s // 2 + 1))
                    for j in range(4):
                        v.bn_stats(bns[:, 12 * j:12 * j + 6],
                                   xt_r[:, s % 2, j, 0:512])
                        v.bn_stats(bns[:, 12 * j + 6:12 * j + 12],
                                   xt_r[:, s % 2, j, 512:1024])
                    v.drain()
                    for j in range(4):
                        v.bn_aggr(mv[:, 2 * (4 * s + j):2 * (4 * s + j) + 2],
                                  bns[:, 12 * j:12 * j + 12])
                    a, b = 4 * s, 4 * s + 4
                    rb = (s % 2) * 4
                    v.drain()
                    v.tensor_scalar(nmt[:, rb:rb + 4], mvr[:, 1, a:b], cstb[:, 2:3],
                                    None, OP.add)
                    v.drain()
                    v.reciprocal(ssq[:, rb:rb + 4], nmt[:, rb:rb + 4]).then_inc(vFIN, 1)
                v.wait_ge(sW0, 16 * 2)
                stats(0)
                for s in range(NS):
                    if s + 1 < NS:
                        stats(s + 1)
                    if s >= 2:
                        v.wait_ge(sSCR, 16 * (s - 1))
                    for j in range(4):
                        tev(v, s % 2, j, 4 * s + j)
                    if s >= 1:
                        v.wait_ge(mmBU, s)
                        v.tensor_scalar(bu[0:N, (s - 1) * ST:s * ST], P[2][0:N, 0:512],
                                        dtb[0:N, 0:1], None, OP.mult).then_inc(vBU, 1)
                v.wait_ge(mmBU, NS)
                v.tensor_scalar(bu[0:N, (NS - 1) * ST:NS * ST], P[2][0:N, 0:512],
                                dtb[0:N, 0:1], None, OP.mult).then_inc(vBU, 1)

            @blk.scalar
            def _(s_):
                for s in range(NS):
                    a, b = 4 * s, 4 * s + 4
                    rb = (s % 2) * 4
                    s_.wait_ge(vFIN, s + 1)
                    s_.activation(rstd[:, a:b], ssq[:, rb:rb + 4], AF.Sqrt)
                    s_.activation(nmt[:, rb:rb + 4], mvr[:, 0, a:b], AF.Identity,
                                  scale=cstb[:, 3:4])
                    s_.drain()
                    for j in range(4):
                        s_.activation(mrn[:, a + j:a + j + 1], rstd[:, a + j:a + j + 1],
                                      AF.Identity, scale=nmt[:, rb + j:rb + j + 1])
                    s_.drain()
                    for j in range(4):
                        if s * 4 + j >= 2:
                            s_.wait_ge(mmT, 4 * s + j - 1)
                        s_.activation(xnb_r[:, j % 2, :], xt_r[:, s % 2, j, :], AF.Identity,
                                      scale=rstd[:, a + j:a + j + 1],
                                      bias=mrn[:, a + j:a + j + 1]).then_inc(xnSC, 1)

            @blk.gpsimd
            def _(g):
                casts = [lambda dk=dk: pdma(g, sW1, w1b_r[:, dk, :],
                                            w1[dk * 128:(dk + 1) * 128, :])
                         for dk in range(DK)]
                casts += [lambda j=j: pdma(g, sW2X,
                                           w2s[j * (H // 8):(j + 1) * (H // 8), :],
                                           w2[j * (H // 8):(j + 1) * (H // 8), :])
                          for j in range(8)]
                ci = 0
                for sp in range(NS):
                    g.wait_ge(vXF, 4 * (sp + 1))
                    pdma(g, sSCR, xscr2[sp * 128:(sp + 1) * 128, :],
                         xfm_r[:, sp % 2, :, :])
                    take = 1 if sp < NS - 1 else len(casts) - ci
                    for _ in range(take):
                        if ci < len(casts):
                            casts[ci]()
                            ci += 1

            def ttile(t, slot, ti_seen, j, xn_gate):
                t.wait_ge(xnSC, xn_gate)
                if ti_seen >= 2:
                    t.wait_ge(vXF, ti_seen - 1)
                bank = PT[ti_seen % 2]
                for dk in range(DK):
                    mm = t.transpose(bank[0:128, dk * 128:(dk + 1) * 128],
                                     xnb_r[:, j % 2, dk * 128:(dk + 1) * 128],
                                     i128b[:])
                mm.then_inc(mmT, 1)

            def buMM(t, s):
                t.wait_ge(vXF, 4 * (s + 1))
                if s >= 1:
                    t.wait_ge(vBU, s)
                for dk in range(DK):
                    mm = t.matmul(P[2][0:128, 0:512], btb_r[:, dk, :],
                                  xfm_r[:, s % 2, dk, :],
                                  start=(dk == 0), stop=(dk == DK - 1))
                mm.then_inc(mmBU, 1)

            @blk.tensor
            def _(t):
                t.wait_ge(sW, NSMALL)
                for s in range(NS):
                    for j in range(4):
                        ttile(t, s % 2, 4 * s + j, j, 4 * s + j + 1)
                    if s >= 1:
                        buMM(t, s - 1)
                buMM(t, NS - 1)

        GK = 8
        NG = CC // GK
        F_r = Sst_r[:, (T - 1) % 2, :].rearrange("n (g k) -> n k g", k=GK)
        I_r = INIT.rearrange("n (g k) -> n k g", k=GK)
        X2a = Sst[:, 0:NG]
        X2b = Sst[:, NG:2 * NG]
        GI = Sst[:, 2 * NG:3 * NG]
        _mm = [0]; _vs = [0]
        sched_t = []
        sched_v = []

        def t_op(fn):
            sched_t.append(fn)

        def v_op(fn):
            sched_v.append(fn)

        def f(v):
            v.tensor_copy(Sst_r[:, 1, :], bu_r[:, 0, :]).then_inc(vS, 1)
        v_op(f); _vs[0] += 1
        for k in range(1, T // 4):
            i0 = 4 * (k - 1)
            def f(t, k=k, i0=i0, w=_vs[0]):
                t.wait_ge(vS, w)
                t.matmul(P[0][0:N, 0:CC], ab4Tb[:], Sst_r[:, k % 2, :],
                         start=True, stop=False)
                t.matmul(P[0][0:N, 0:CC], ab3Tb[:], bu_r[:, i0 + 1, :],
                         start=False, stop=False)
                t.matmul(P[0][0:N, 0:CC], ab2Tb[:], bu_r[:, i0 + 2, :],
                         start=False, stop=False)
                t.matmul(P[0][0:N, 0:CC], abTB[:], bu_r[:, i0 + 3, :],
                         start=False, stop=True).then_inc(mmS, 1)
            t_op(f); _mm[0] += 1
            def f(v, k=k, i0=i0, w=_mm[0]):
                v.wait_ge(mmS, w)
                v.tensor_tensor(Sst_r[:, (k + 1) % 2, :], P[0][0:N, 0:CC],
                                bu_r[:, i0 + 4, :], OP.add).then_inc(vS, 1)
            v_op(f); _vs[0] += 1
        def f(t, w=_vs[0]):
            t.wait_ge(vS, w)
            t.matmul(P[0][0:N, 0:CC], ab3Tb[:], Sst_r[:, 0, :],
                     start=True, stop=False)
            t.matmul(P[0][0:N, 0:CC], ab2Tb[:], bu_r[:, T - 3, :],
                     start=False, stop=False)
            t.matmul(P[0][0:N, 0:CC], abTB[:], bu_r[:, T - 2, :],
                     start=False, stop=True).then_inc(mmS, 1)
        t_op(f); _mm[0] += 1
        def f(v, w=_mm[0]):
            v.wait_ge(mmS, w)
            v.tensor_tensor(Sst_r[:, 1, :], P[0][0:N, 0:CC],
                            bu_r[:, T - 1, :], OP.add).then_inc(vS, 1)
        v_op(f); _vs[0] += 1
        def f(v):
            v.drain()
            v.tensor_copy(X2a, F_r[:, 0, :]).then_inc(vS, 1)
        v_op(f); _vs[0] += 1
        Xs = (X2a, X2b)
        for k in range(1, GK):
            def f(t, k=k, w=_vs[0]):
                t.wait_ge(vS, w)
                t.matmul(P[0][0:N, 0:NG], p64T[:], Xs[(k - 1) % 2],
                         start=True, stop=True).then_inc(mmS, 1)
            t_op(f); _mm[0] += 1
            def f(v, k=k, w=_mm[0]):
                v.wait_ge(mmS, w)
                v.tensor_tensor(Xs[k % 2], P[0][0:N, 0:NG],
                                F_r[:, k, :], OP.add).then_inc(vS, 1)
            v_op(f); _vs[0] += 1
        Lsl = Xs[(GK - 1) % 2]
        def f(v):
            v.drain()
            v.memset(GI[:, 0:1], 0.0).then_inc(vS, 1)
        v_op(f); _vs[0] += 1
        for g in range(1, NG):
            def f(t, g=g, w=_vs[0]):
                t.wait_ge(vS, w)
                t.matmul(P[1][0:N, 0:1], P512T[:], GI[:, g - 1:g],
                         start=True, stop=True).then_inc(mmS, 1)
            t_op(f); _mm[0] += 1
            def f(v, g=g, w=_mm[0]):
                v.wait_ge(mmS, w)
                v.tensor_tensor(GI[:, g:g + 1], P[1][0:N, 0:1],
                                Lsl[:, g - 1:g], OP.add).then_inc(vS, 1)
            v_op(f); _vs[0] += 1
        def f(v):
            v.drain()
            v.memset(INIT[:, 0:1], 0.0).then_inc(vS, 1)
        v_op(f); _vs[0] += 1
        for k in range(GK):
            def f(t, k=k, w=_vs[0]):
                t.wait_ge(vS, w)
                rhs = GI if k == 0 else Xs[(k - 1) % 2]
                t.matmul(P[0][0:N, 0:NG], p64T[:], rhs,
                         start=True, stop=True).then_inc(mmS, 1)
            t_op(f); _mm[0] += 1
            def f(v, k=k, w=_mm[0]):
                v.wait_ge(mmS, w)
                v.tensor_tensor(Xs[k % 2], P[0][0:N, 0:NG],
                                F_r[:, k, :], OP.add)
                v.drain()
                if k < GK - 1:
                    v.tensor_copy(I_r[:, k + 1, :], Xs[k % 2]).then_inc(vS, 1)
                else:
                    v.tensor_copy(I_r[:, 0, 1:NG], Xs[k % 2][:, 0:NG - 1]).then_inc(vS, 1)
            v_op(f); _vs[0] += 1
        def mk_ffn1_unit(j):
            def f(t, j=j):
                if j == 0:
                    t.wait_ge(sTR, 16)
                    t.wait_ge(sW1, 16 * DK)
                if j - 1 > 0:
                    t.wait_ge(geSC, j - 1)
                for dk in range(DK):
                    mm = t.matmul(P[2 + j % 2][0:128, 0:ST],
                                  w1b_r[:, dk, j * 128:(j + 1) * 128],
                                  xfm_r[:, 0, dk, :],
                                  start=(dk == 0), stop=(dk == DK - 1))
                mm.then_inc(mmH, 1)
            return f

        n_pre = len(sched_t)
        sched_t2 = []
        ins_pos = {max(1, (k + 1) * n_pre // HJ): k for k in range(HJ)}
        for idx, fn in enumerate(sched_t):
            sched_t2.append(fn)
            if idx + 1 in ins_pos:
                sched_t2.append(mk_ffn1_unit(ins_pos[idx + 1]))
        sched_t = sched_t2

        pb_cnt = []
        for k in range(T // 2):
            def f(t, k=k, w=_vs[0]):
                t.wait_ge(vS, w)
                src = INIT[:, :] if k == 0 else Sst_r[:, (k - 1) % 2, :]
                t.matmul(P[1][0:N, 0:CC], abTB[:], src,
                         start=True, stop=True).then_inc(mmS, 1)
                t.matmul(P[0][0:N, 0:CC], ab2Tb[:], src,
                         start=True, stop=False)
                t.matmul(P[0][0:N, 0:CC], abTB[:], bu_r[:, 2 * k, :],
                         start=False, stop=True).then_inc(mmS, 1)
            t_op(f); _mm[0] += 2
            def f(v, k=k, w=_mm[0]):
                v.wait_ge(mmS, w)
                if k >= 1:
                    v.wait_ge(xsSC, 2 * k - 1)
                v.tensor_tensor(INIT[:, :], P[1][0:N, 0:CC],
                                bu_r[:, 2 * k, :], OP.add).then_inc(vS, 1)
                if k >= 2:
                    v.wait_ge(xsSC, 2 * k - 2)
                v.tensor_tensor(Sst_r[:, k % 2, :], P[0][0:N, 0:CC],
                                bu_r[:, 2 * k + 1, :], OP.add).then_inc(vS, 1)
            v_op(f)
            _vs[0] += 2
            pb_cnt.append((_vs[0] - 1, _vs[0]))

        with nc.Block() as blk:
            @blk.tensor
            def _(t):
                for fn in sched_t:
                    fn(t)

            @blk.vector
            def _(v):
                for fn in sched_v:
                    fn(v)

            @blk.gpsimd
            def _(g):
                g.wait_ge(sW2X, 16 * 8)
                for j0 in range(0, 16, 4):
                    pdma(g, sW2, w2b_r[:, j0:j0 + 4, :], w2s_r[:, j0:j0 + 4, 0:512])

            @blk.scalar
            def _(s_):
                s_.wait_ge(sW, NSMALL)
                for j in range(HJ):
                    s_.wait_ge(mmH, j + 1)
                    s_.activation(hb_r[:, j, :], P[2 + j % 2][0:128, 0:ST], AF.Gelu,
                                  bias=b1tb[:, j:j + 1]).then_inc(geSC, 1)
                for k in range(T // 2):
                    s_.wait_ge(vS, pb_cnt[k][0])
                    s_.copy(bu_r[:, 2 * k, :], INIT[:, :]).then_inc(xsSC, 1)
                    s_.wait_ge(vS, pb_cnt[k][1])
                    s_.copy(bu_r[:, 2 * k + 1, :],
                            Sst_r[:, k % 2, :]).then_inc(xsSC, 1)

        TRBASE = 16 * DK * 4 * NS

        with nc.Block() as blk:
            @blk.sync
            def _(sy):
                for s in range(1, NS):
                    if s >= 2:
                        sy.wait_ge(vXT, 4 * (s - 1))
                    pdma(sy, sX2, xt_r[:, s % 2, :, :], x_r[:, s, :, :])
                    if s == 1:
                        sy.wait_ge(sSCR, 16 * NS)
                    if s >= 2:
                        sy.wait_ge(mmH, HJ * (s - 1))
                    pdma(sy, sTR, xfm_r[:, s % 2, :, :],
                         xscr2[s * 128:(s + 1) * 128, :])

            @blk.gpsimd
            def _(g):
                gidx = 0
                for s in range(NS):
                    for dh in range(2):
                        for j0 in range(0, HJ, 4):
                            if gidx < 16:
                                gidx += 4
                                continue
                            g.wait_ge(w2c, gidx - 12)
                            sl = gidx % 16
                            pdma(g, sW2, w2b_r[:, sl:sl + 4, :],
                                 w2s_r[:, j0:j0 + 4, dh * 512:(dh + 1) * 512])
                            gidx += 4

            TPOS = {2: 0, 10: 1, 18: 2, 26: 3}

            @blk.tensor
            def _(t):
                t.wait_ge(sW, NSMALL)
                for s in range(NS):
                    if s > 0:
                        t.wait_ge(sTR, 16 * (s + 1))
                        for j in range(HJ):
                            gate = HJ * s + j - 1
                            if gate > 0:
                                t.wait_ge(geSC, gate)
                            for dk in range(DK):
                                mm = t.matmul(P[j % 2][0:128, 0:ST],
                                              w1b_r[:, dk, j * 128:(j + 1) * 128],
                                              xfm_r[:, s % 2, dk, :],
                                              start=(dk == 0), stop=(dk == DK - 1))
                            mm.then_inc(mmH, 1)
                    t.wait_ge(geSC, HJ * (s + 1))
                    for dh in range(2):
                        G0 = s * 8 + dh * 4
                        banks = [2 + ((G0 + q) % 4) for q in range(4)]
                        for j in range(HJ):
                            gch = (s * 2 + dh) * HJ + j
                            t.wait_ge(sW2, 16 * (gch // 4 + 1))
                            sl = gch % 16
                            for q in range(4):
                                if j == 0 and G0 + q >= 4:
                                    t.wait_ge(vEV, G0 + q - 4 + 1)
                                mm = t.matmul(P[banks[q]][0:128, 0:512],
                                              hb_r[:, j, q * 128:(q + 1) * 128],
                                              w2b_r[:, sl, :], start=(j == 0),
                                              stop=False)
                            mm.then_inc(w2c, 1)
                        for q in range(4):
                            t.matmul(P[banks[q]][0:128, 0:512],
                                     bu[0:N + 1, (4 * s + q) * TT:(4 * s + q + 1) * TT],
                                     ctb[0:N + 1, dh * 512:(dh + 1) * 512],
                                     start=False, stop=True).then_inc(mmF, 1)

            @blk.scalar
            def _(s_):
                for s in range(NS):
                    if s > 0:
                        for j in range(HJ):
                            s_.wait_ge(mmH, HJ * s + j + 1)
                            s_.activation(hb_r[:, j, :], P[j % 2][0:128, 0:ST],
                                          AF.Gelu,
                                          bias=b1tb[:, j:j + 1]).then_inc(geSC, 1)
                    for q in range(4):
                        s_.wait_ge(vEV, 8 * s + 4 + q + 1)
                        pdma(s_, sOUT, o[(4 * s + q) * TT:(4 * s + q + 1) * TT, :],
                             outb_r[:, q, :])
                s_.wait_ge(sOUT, 16 * 4 * NS)

            @blk.vector
            def _(v):
                v.wait_ge(sW, NSMALL)
                for s in range(NS):
                    v.wait_ge(sX2, 16 * (s + 1))
                    for q in range(4):
                        if s >= 1 and q == 0:
                            v.wait_ge(sOUT, 16 * 4 * s)
                        col = 4 * s + q
                        v.tensor_scalar(outb_r[:, q, :], xt_r[:, s % 2, q, :],
                                        rstd[:, col:col + 1], mrn[:, col:col + 1],
                                        OP.mult, OP.add)
                        v.drain()
                        v.tensor_tensor(outb_r[:, q, :], outb_r[:, q, :], dsb[:],
                                        OP.mult)
                        v.drain()
                        v.tensor_tensor(outb_r[:, q, :], outb_r[:, q, :],
                                        xt_r[:, s % 2, q, :], OP.add).then_inc(vXT, 1)
                        v.drain()
                    for dh in range(2):
                        G0 = s * 8 + dh * 4
                        banks = [2 + ((G0 + q) % 4) for q in range(4)]
                        for q in range(4):
                            v.wait_ge(mmF, G0 + q + 1)
                            v.tensor_tensor(outb_r[:, q, dh * 512:(dh + 1) * 512],
                                            outb_r[:, q, dh * 512:(dh + 1) * 512],
                                            P[banks[q]][0:128, 0:512],
                                            OP.add).then_inc(vEV, 1)

        if debug_taps:
            with nc.Block() as blk:
                dbg_sem = sem("dbg_sem")
                outs = []
                for name in debug_taps:
                    h = tens[name]
                    outs.append((nc.dram_tensor(f"dbg_{name}", list(h.shape),
                                                h.dtype, kind="ExternalOutput").ap(), h))

                @blk.gpsimd
                def _(g):
                    n = 0
                    for dst, h in outs:
                        pdma(g, dbg_sem, dst[:, :], h[:])
                        n += 16
                    g.wait_ge(dbg_sem, n)

    return nc


def _host_inputs(ins):
    bf = ml_dtypes.bfloat16
    A = np.ascontiguousarray(ins["L0_attn_A"], dtype=np.float32)
    B = np.ascontiguousarray(ins["L0_attn_B"], dtype=np.float32)
    C = np.ascontiguousarray(ins["L0_attn_C"], dtype=np.float32)
    Dv = np.ascontiguousarray(ins["L0_attn_D"], dtype=np.float32)
    logdt = np.ascontiguousarray(ins["L0_attn_log_dt"], dtype=np.float32)
    w1 = np.ascontiguousarray(ins["L0_ffn_w1"], dtype=np.float32)
    b1 = np.ascontiguousarray(ins["L0_ffn_b1"], dtype=np.float32)
    w2 = np.ascontiguousarray(ins["L0_ffn_w2"], dtype=np.float32)
    b2 = np.ascontiguousarray(ins["L0_ffn_b2"], dtype=np.float32)
    return dict(
        w1=w1,
        w2=w2,
        bT=np.ascontiguousarray(np.pad(B.T, ((0, 0), (0, 128 - N)))),
        cT=np.ascontiguousarray(np.vstack([C.T, b2.reshape(1, D)])),
        aN=A,
        aT=np.ascontiguousarray(A.T),
        i64=np.eye(N, dtype=np.float32),
        d_rep=np.ascontiguousarray(np.broadcast_to(Dv, (128, D))),
        ldr=np.ascontiguousarray(np.broadcast_to(logdt, (128, D))),
        b1t=np.ascontiguousarray(b1.reshape(HJ, 128).T),
        eye128=np.eye(128, dtype=bf),
        ones8k=np.ones((1, ins["L0_attn_u"].shape[1]), dtype=bf),
        cst=np.ascontiguousarray(np.broadcast_to(
            np.array([1.0 / D, -1.0 / D, 1e-5, -1.0], dtype=np.float32), (128, 4))),
    )


def kernel(L0_attn_u, L0_attn_A, L0_attn_B, L0_attn_C, L0_attn_D, L0_attn_log_dt,
           L0_ffn_w1, L0_ffn_b1, L0_ffn_w2, L0_ffn_b2):
    ins = dict(L0_attn_u=L0_attn_u, L0_attn_A=L0_attn_A, L0_attn_B=L0_attn_B,
               L0_attn_C=L0_attn_C, L0_attn_D=L0_attn_D,
               L0_attn_log_dt=L0_attn_log_dt, L0_ffn_w1=L0_ffn_w1,
               L0_ffn_b1=L0_ffn_b1, L0_ffn_w2=L0_ffn_w2, L0_ffn_b2=L0_ffn_b2)
    u = np.ascontiguousarray(np.asarray(L0_attn_u, dtype=np.float32))
    Bn, L, _ = u.shape
    shared = _host_inputs(ins)
    nc = build_nc(L)
    in_maps = [dict(x=np.ascontiguousarray(u[i]), **shared) for i in range(Bn)]
    res = run_bass_kernel_spmd(nc, in_maps, core_ids=list(range(Bn)))
    return np.stack([r["o"] for r in res.results], axis=0)


# version 40
# speedup vs baseline: 1.0099x; 1.0099x over previous
import numpy as np
import ml_dtypes
import concourse.bass as bass
import concourse.mybir as mybir
from concourse.bass_utils import run_bass_kernel_spmd
from contextlib import ExitStack

F32 = mybir.dt.float32
BF16 = mybir.dt.bfloat16
OP = mybir.AluOpType
AF = mybir.ActivationFunctionType
AX = mybir.AxisListType.X

D = 1024
H = 4096
N = 64
ST = 512
TT = 128
T = 64
DK = D // 128
HJ = H // 128


def build_nc(L, debug_taps=(), paranoid=False):
    NS = L // ST
    CC = L // T
    n_ttile = L // TT

    nc = bass.Bass("TRN2", target_bir_lowering=False)

    x = nc.dram_tensor("x", [L, D], F32, kind="ExternalInput").ap()
    w1 = nc.dram_tensor("w1", [D, H], F32, kind="ExternalInput").ap()
    w2 = nc.dram_tensor("w2", [H, D], F32, kind="ExternalInput").ap()
    bT = nc.dram_tensor("bT", [D, 128], F32, kind="ExternalInput").ap()
    cT = nc.dram_tensor("cT", [N + 1, D], F32, kind="ExternalInput").ap()
    aN = nc.dram_tensor("aN", [N, N], F32, kind="ExternalInput").ap()
    aT = nc.dram_tensor("aT", [N, N], F32, kind="ExternalInput").ap()
    i64 = nc.dram_tensor("i64", [N, N], F32, kind="ExternalInput").ap()
    d_rep = nc.dram_tensor("d_rep", [128, D], F32, kind="ExternalInput").ap()
    ldr = nc.dram_tensor("ldr", [128, D], F32, kind="ExternalInput").ap()
    b1t = nc.dram_tensor("b1t", [128, HJ], F32, kind="ExternalInput").ap()
    ones8k = nc.dram_tensor("ones8k", [1, L], BF16, kind="ExternalInput").ap()
    eye128 = nc.dram_tensor("eye128", [128, 128], BF16, kind="ExternalInput").ap()
    cst = nc.dram_tensor("cst", [128, 4], F32, kind="ExternalInput").ap()
    o = nc.dram_tensor("o", [L, D], F32, kind="ExternalOutput").ap()
    w2s = nc.dram_tensor("w2s", [H, D], BF16, kind="Internal").ap()
    xscr2 = nc.dram_tensor("xscr2", [L // ST * 128, DK * ST], BF16, kind="Internal").ap()

    with ExitStack() as ctx:
        tens = {}
        def sb(name, shape, dt=F32):
            h = ctx.enter_context(nc.sbuf_tensor(name, shape, dt))
            tens[name] = h
            return h
        def psd(name):
            return ctx.enter_context(nc.psum_tensor(name, [128, 512], F32))
        def sem(name):
            return ctx.enter_context(nc.semaphore(name))

        w1b = sb("w1b", [128, DK * H], BF16)
        xt = sb("xt", [128, 2 * (ST // TT) * D])
        xnb = sb("xnb", [128, 2 * D], BF16)
        xfm = sb("xfm", [128, 2 * DK * ST], BF16)
        bu = sb("bu", [N + 1, L], BF16)
        hb = sb("hb", [128, HJ * ST], BF16)
        outb = sb("outb", [128, 4 * D])
        w2b = sb("w2b", [128, 16 * 512], BF16)
        dsb = sb("dsb", [128, D])
        bns = sb("bns", [128, 48])
        ssq = sb("ssq", [128, 8])
        rstd = sb("rstd", [128, n_ttile])
        mrn = sb("mrn", [128, n_ttile])
        nmt = sb("nmt", [128, 8])
        btb = sb("btb", [128, DK * 128], BF16)
        ctb = sb("ctb", [N + 1, D], BF16)
        b1tb = sb("b1tb", [128, HJ])
        dtb = sb("dtb", [128, 1])
        cstb = sb("cstb", [128, 4])
        abB = sb("abB", [N, N], BF16); abTB = sb("abTB", [N, N], BF16)
        zb0 = sb("zb0", [N, N], BF16); zb1 = sb("zb1", [N, N], BF16)
        zbT0 = sb("zbT0", [N, N], BF16); zbT1 = sb("zbT1", [N, N], BF16)
        p64T = sb("p64T", [N, N], BF16)
        ab2Tb = sb("ab2Tb", [N, N], BF16); ab3Tb = sb("ab3Tb", [N, N], BF16)
        ab4Tb = sb("ab4Tb", [N, N], BF16)
        Sst = sb("Sst", [N, 2 * CC], BF16)
        i128b = sb("i128b", [128, 128], BF16)
        INIT = sb("INIT", [N, CC], BF16)

        P = [psd(f"p{i}") for i in range(6)]
        PT = [ctx.enter_context(nc.psum_tensor(f"pt{i}", [128, 1024], BF16))
              for i in range(2)]

        sIN = sem("sIN"); sSCR = sem("sSCR"); sTR = sem("sTR"); sX2 = sem("sX2")
        sW2 = sem("sW2"); sW = sem("sW"); sOUT = sem("sOUT")
        mmBU = sem("mmBU"); mmS = sem("mmS"); mmH = sem("mmH"); mmF = sem("mmF")
        w2c = sem("w2c"); vS = sem("vS"); vBU = sem("vBU"); vFIN = sem("vFIN")
        vEV = sem("vEV"); vXT = sem("vXT"); sqSC = sem("sqSC"); xnSC = sem("xnSC")
        geSC = sem("geSC"); xsSC = sem("xsSC")
        sW1 = sem("sW1"); sW2X = sem("sW2X")
        semV0 = sem("semV0"); semA0 = sem("semA0"); semT0 = sem("semT0")
        mmT = sem("mmT"); vXF = sem("vXF")
        sW0 = sem("sW0"); sWA = sem("sWA"); sIN2 = sem("sIN2")
        sSCR2 = sem("sSCR2"); sTR2 = sem("sTR2"); sX22 = sem("sX22")
        w2q = [sem(f"w2q{i}") for i in range(4)]

        all_sems = [sIN, sSCR, sTR, sX2, sW2, sW, sOUT, mmBU, mmS, mmH, mmF,
                    w2c, vS, vBU, vFIN, vEV, vXT, sqSC, xnSC, geSC, xsSC,
                    sW1, sW2X, semV0, semA0, semT0, mmT, vXF, sW0, sWA, sIN2,
                    sSCR2, sTR2, sX22] + w2q
        sem_nums = [s.num for s in all_sems]
        lo, hi = min(sem_nums), max(sem_nums)
        assert hi - lo + 1 == len(sem_nums), "semaphore range not contiguous"
        with nc.Block() as blk:
            @blk.gpsimd
            def _(g):
                g.dma_reset(range(lo, hi + 1))
                g.sem_clear(range(lo, hi + 1))

        _dmacnt = {}
        def pdma(eng, sm, out_ap, in_ap, **kw):
            c = _dmacnt.get(sm.num, 0)
            if paranoid and c > 0:
                eng.wait_ge(sm, c)
            eng.dma_start(out_ap, in_ap, **kw).then_inc(sm, 16)
            _dmacnt[sm.num] = c + 16

        xt_r = xt.rearrange("p (r j d) -> p r j d", r=2, d=D)
        xfm_r = xfm.rearrange("p (r k t) -> p r k t", r=2, t=ST)
        xfm_r2 = xfm.rearrange("p (r k j t) -> p r k j t", r=2, k=DK, t=TT)
        xnb_r = xnb.rearrange("p (r d) -> p r d", r=2)
        hb_r = hb.rearrange("p (j t) -> p j t", t=ST)
        outb_r = outb.rearrange("p (q d) -> p q d", d=D)
        w2b_r = w2b.rearrange("p (s f) -> p s f", s=16)
        w1b_r = w1b.rearrange("p (k h) -> p k h", k=DK)
        btb_r = btb.rearrange("p (k n) -> p k n", k=DK)
        bu_r = bu[0:N, :].rearrange("n (c i) -> n i c", i=T)
        Sst_r = Sst.rearrange("n (r c) -> n r c", r=2)
        x_r = x.rearrange("(s j p) d -> p s j d", j=ST // TT, p=TT)
        w2s_r = w2s.rearrange("(j p) d -> p j d", p=128)

        mv = outb[0:128, D + 4 * N:D + 4 * N + 2 * n_ttile]
        aNs = outb[0:N, D:D + N]
        aTs = outb[0:N, D + N:D + 2 * N]
        i64s = outb[0:N, D + 2 * N:D + 3 * N]
        zf = outb[0:N, D + 3 * N:D + 4 * N]

        NSMALL = 16 * 6

        with nc.Block() as blk:
            @blk.gpsimd
            def _(g):
                pdma(g, sW0, outb[:, 0:D], ldr[:, :])
                pdma(g, sW0, cstb[:], cst[:, :])
                pdma(g, sWA, aTs, aT[:, :])
                pdma(g, sWA, i64s, i64[:, :])
                pdma(g, sWA, aNs, aN[:, :])
                for dst, src in ((dsb[:], d_rep[:, :]), (b1tb[:], b1t[:, :]),
                                 (bu[N:N + 1, :], ones8k[:, :]),
                                 (i128b[:], eye128[:, :]), (ctb[:], cT[:, :])):
                    pdma(g, sW, dst, src)
                pdma(g, sW, btb[:], bT.rearrange("(k p) n -> p k n", p=128))

            @blk.vector
            def _(v):
                v.wait_ge(semA0, 1)
                v.reduce_sum(dtb[:], outb[:, 0:D], axis=AX)
                v.drain()
                v.tensor_scalar(dtb[:], dtb[:], cstb[:, 0:1], None, OP.mult)
                v.drain()
                v.wait_ge(sWA, 16 * 3)
                v.scalar_tensor_tensor(zf, aTs, dtb[0:N, 0:1], i64s,
                                       OP.mult, OP.add).then_inc(semV0, 1)
                v.wait_ge(semA0, 2)
                v.scalar_tensor_tensor(zf, aNs, dtb[0:N, 0:1], i64s,
                                       OP.mult, OP.add).then_inc(semV0, 1)

            @blk.scalar
            def _(s):
                s.wait_ge(sW0, 16 * 2)
                s.activation(outb[:, 0:D], outb[:, 0:D], AF.Exp).then_inc(semA0, 1)
                s.wait_ge(semV0, 1)
                s.copy(abTB[:], zf).then_inc(semA0, 1)
                s.wait_ge(semV0, 2)
                s.copy(abB[:], zf).then_inc(semA0, 1)
                for k in range(9):
                    zT_dst = (zbT0, zbT1)[k % 2]
                    z_dst = (zb0, zb1)[k % 2]
                    s.wait_ge(semT0, 2 * (k + 1))
                    s.copy(zT_dst[:], P[0][0:N, 0:N])
                    if k == 0:
                        s.copy(ab2Tb[:], P[0][0:N, 0:N])
                    if k == 1:
                        s.copy(ab4Tb[:], P[0][0:N, 0:N])
                    if k == 5:
                        s.copy(p64T[:], P[0][0:N, 0:N])
                    s.copy(z_dst[:], P[1][0:N, 0:N]).then_inc(semA0, 1)
                s.wait_ge(semT0, 19)
                s.copy(ab3Tb[:], P[0][0:N, 0:N])

            @blk.sync
            def _(sy):
                for sp in range(2):
                    for j in range(4):
                        pdma(sy, (sIN, sIN2)[sp % 2], xt_r[:, sp, j, :],
                             x_r[:, sp, j, :])

            @blk.tensor
            def _(t):
                t.wait_ge(semA0, 3)
                cur_z, cur_zT = abB, abTB
                for k in range(9):
                    if k > 0:
                        t.wait_ge(semA0, 3 + k)
                    t.matmul(P[0][0:N, 0:N], cur_z[:], cur_zT[:],
                             start=True, stop=True).then_inc(semT0, 1)
                    t.matmul(P[1][0:N, 0:N], cur_zT[:], cur_z[:],
                             start=True, stop=True).then_inc(semT0, 1)
                    cur_z, cur_zT = (zb0, zb1)[k % 2], (zbT0, zbT1)[k % 2]
                t.wait_ge(semA0, 12)
                t.matmul(P[0][0:N, 0:N], abB[:], ab2Tb[:],
                         start=True, stop=True).then_inc(semT0, 1)
        P512T = zbT0

        with nc.Block() as blk:
            @blk.sync
            def _(sy):
                for s in range(2, NS):
                    for j in range(4):
                        sy.wait_ge(xnSC, 4 * (s - 2) + j + 1)
                        pdma(sy, (sIN, sIN2)[s % 2], xt_r[:, s % 2, j, :],
                             x_r[:, s, j, :])
                sy.wait_ge(xnSC, 4 * (NS - 1))
                pdma(sy, sX2, xt_r[:, 0, :, :], x_r[:, 0, :, :])
                sy.wait_ge(sSCR, 16 * ((NS - 2) // 2 + 1))
                sy.wait_ge(mmBU, NS - 1)
                pdma(sy, sTR, xfm_r[:, 0, :, :], xscr2[0:128, :])

            mvr = mv.rearrange("p (j two) -> p two j", two=2)

            def tev(v, slot, j, ti_seen):
                v.wait_ge(mmT, ti_seen + 1)
                v.tensor_copy(xfm_r2[:, slot, :, j, :],
                              PT[ti_seen % 2][0:128, 0:1024]).then_inc(vXF, 1)

            @blk.vector
            def _(v):
                def stats(s):
                    v.drain()
                    v.wait_ge((sIN, sIN2)[s % 2], 16 * 4 * (s // 2 + 1))
                    for j in range(4):
                        v.bn_stats(bns[:, 12 * j:12 * j + 6],
                                   xt_r[:, s % 2, j, 0:512])
                        v.bn_stats(bns[:, 12 * j + 6:12 * j + 12],
                                   xt_r[:, s % 2, j, 512:1024])
                    v.drain()
                    for j in range(4):
                        v.bn_aggr(mv[:, 2 * (4 * s + j):2 * (4 * s + j) + 2],
                                  bns[:, 12 * j:12 * j + 12])
                    a, b = 4 * s, 4 * s + 4
                    rb = (s % 2) * 4
                    v.drain()
                    v.tensor_scalar(nmt[:, rb:rb + 4], mvr[:, 1, a:b], cstb[:, 2:3],
                                    None, OP.add)
                    v.drain()
                    v.reciprocal(ssq[:, rb:rb + 4], nmt[:, rb:rb + 4]).then_inc(vFIN, 1)
                v.wait_ge(sW0, 16 * 2)
                stats(0)
                for s in range(NS):
                    if s + 1 < NS:
                        stats(s + 1)
                    if s >= 2:
                        v.wait_ge((sSCR, sSCR2)[s % 2],
                                  16 * ((s - 2) // 2 + 1))
                    for j in range(4):
                        tev(v, s % 2, j, 4 * s + j)
                    if s >= 1:
                        v.wait_ge(mmBU, s)
                        v.tensor_scalar(bu[0:N, (s - 1) * ST:s * ST], P[2][0:N, 0:512],
                                        dtb[0:N, 0:1], None, OP.mult).then_inc(vBU, 1)
                v.wait_ge(mmBU, NS)
                v.tensor_scalar(bu[0:N, (NS - 1) * ST:NS * ST], P[2][0:N, 0:512],
                                dtb[0:N, 0:1], None, OP.mult).then_inc(vBU, 1)

            @blk.scalar
            def _(s_):
                for s in range(NS):
                    a, b = 4 * s, 4 * s + 4
                    rb = (s % 2) * 4
                    s_.wait_ge(vFIN, s + 1)
                    s_.activation(rstd[:, a:b], ssq[:, rb:rb + 4], AF.Sqrt)
                    s_.activation(nmt[:, rb:rb + 4], mvr[:, 0, a:b], AF.Identity,
                                  scale=cstb[:, 3:4])
                    s_.drain()
                    for j in range(4):
                        s_.activation(mrn[:, a + j:a + j + 1], rstd[:, a + j:a + j + 1],
                                      AF.Identity, scale=nmt[:, rb + j:rb + j + 1])
                    s_.drain()
                    for j in range(4):
                        if s * 4 + j >= 2:
                            s_.wait_ge(mmT, 4 * s + j - 1)
                        s_.activation(xnb_r[:, j % 2, :], xt_r[:, s % 2, j, :], AF.Identity,
                                      scale=rstd[:, a + j:a + j + 1],
                                      bias=mrn[:, a + j:a + j + 1]).then_inc(xnSC, 1)

            @blk.gpsimd
            def _(g):
                casts = [lambda dk=dk: pdma(g, sW1, w1b_r[:, dk, :],
                                            w1[dk * 128:(dk + 1) * 128, :])
                         for dk in range(DK)]
                casts += [lambda j=j: pdma(g, sW2X,
                                           w2s[j * (H // 8):(j + 1) * (H // 8), :],
                                           w2[j * (H // 8):(j + 1) * (H // 8), :])
                          for j in range(8)]
                ci = 0
                for sp in range(NS):
                    g.wait_ge(vXF, 4 * (sp + 1))
                    pdma(g, (sSCR, sSCR2)[sp % 2],
                         xscr2[sp * 128:(sp + 1) * 128, :],
                         xfm_r[:, sp % 2, :, :])
                    take = 1 if sp < NS - 1 else len(casts) - ci
                    for _ in range(take):
                        if ci < len(casts):
                            casts[ci]()
                            ci += 1

            def ttile(t, slot, ti_seen, j, xn_gate):
                t.wait_ge(xnSC, xn_gate)
                if ti_seen >= 2:
                    t.wait_ge(vXF, ti_seen - 1)
                bank = PT[ti_seen % 2]
                for dk in range(DK):
                    mm = t.transpose(bank[0:128, dk * 128:(dk + 1) * 128],
                                     xnb_r[:, j % 2, dk * 128:(dk + 1) * 128],
                                     i128b[:])
                mm.then_inc(mmT, 1)

            def buMM(t, s):
                t.wait_ge(vXF, 4 * (s + 1))
                if s >= 1:
                    t.wait_ge(vBU, s)
                for dk in range(DK):
                    mm = t.matmul(P[2][0:128, 0:512], btb_r[:, dk, :],
                                  xfm_r[:, s % 2, dk, :],
                                  start=(dk == 0), stop=(dk == DK - 1))
                mm.then_inc(mmBU, 1)

            @blk.tensor
            def _(t):
                t.wait_ge(sW, NSMALL)
                for s in range(NS):
                    for j in range(4):
                        ttile(t, s % 2, 4 * s + j, j, 4 * s + j + 1)
                    if s >= 1:
                        buMM(t, s - 1)
                buMM(t, NS - 1)

        GK = 8
        NG = CC // GK
        F_r = Sst_r[:, (T - 1) % 2, :].rearrange("n (g k) -> n k g", k=GK)
        I_r = INIT.rearrange("n (g k) -> n k g", k=GK)
        X2a = Sst[:, 0:NG]
        X2b = Sst[:, NG:2 * NG]
        GI = Sst[:, 2 * NG:3 * NG]
        _mm = [0]; _vs = [0]
        sched_t = []
        sched_v = []

        def t_op(fn):
            sched_t.append(fn)

        def v_op(fn):
            sched_v.append(fn)

        def f(v):
            v.tensor_copy(Sst_r[:, 1, :], bu_r[:, 0, :]).then_inc(vS, 1)
        v_op(f); _vs[0] += 1
        for k in range(1, T // 4):
            i0 = 4 * (k - 1)
            def f(t, k=k, i0=i0, w=_vs[0]):
                t.wait_ge(vS, w)
                t.matmul(P[0][0:N, 0:CC], ab4Tb[:], Sst_r[:, k % 2, :],
                         start=True, stop=False)
                t.matmul(P[0][0:N, 0:CC], ab3Tb[:], bu_r[:, i0 + 1, :],
                         start=False, stop=False)
                t.matmul(P[0][0:N, 0:CC], ab2Tb[:], bu_r[:, i0 + 2, :],
                         start=False, stop=False)
                t.matmul(P[0][0:N, 0:CC], abTB[:], bu_r[:, i0 + 3, :],
                         start=False, stop=True).then_inc(mmS, 1)
            t_op(f); _mm[0] += 1
            def f(v, k=k, i0=i0, w=_mm[0]):
                v.wait_ge(mmS, w)
                v.tensor_tensor(Sst_r[:, (k + 1) % 2, :], P[0][0:N, 0:CC],
                                bu_r[:, i0 + 4, :], OP.add).then_inc(vS, 1)
            v_op(f); _vs[0] += 1
        def f(t, w=_vs[0]):
            t.wait_ge(vS, w)
            t.matmul(P[0][0:N, 0:CC], ab3Tb[:], Sst_r[:, 0, :],
                     start=True, stop=False)
            t.matmul(P[0][0:N, 0:CC], ab2Tb[:], bu_r[:, T - 3, :],
                     start=False, stop=False)
            t.matmul(P[0][0:N, 0:CC], abTB[:], bu_r[:, T - 2, :],
                     start=False, stop=True).then_inc(mmS, 1)
        t_op(f); _mm[0] += 1
        def f(v, w=_mm[0]):
            v.wait_ge(mmS, w)
            v.tensor_tensor(Sst_r[:, 1, :], P[0][0:N, 0:CC],
                            bu_r[:, T - 1, :], OP.add).then_inc(vS, 1)
        v_op(f); _vs[0] += 1
        def f(v):
            v.drain()
            v.tensor_copy(X2a, F_r[:, 0, :]).then_inc(vS, 1)
        v_op(f); _vs[0] += 1
        Xs = (X2a, X2b)
        for k in range(1, GK):
            def f(t, k=k, w=_vs[0]):
                t.wait_ge(vS, w)
                t.matmul(P[0][0:N, 0:NG], p64T[:], Xs[(k - 1) % 2],
                         start=True, stop=True).then_inc(mmS, 1)
            t_op(f); _mm[0] += 1
            def f(v, k=k, w=_mm[0]):
                v.wait_ge(mmS, w)
                v.tensor_tensor(Xs[k % 2], P[0][0:N, 0:NG],
                                F_r[:, k, :], OP.add).then_inc(vS, 1)
            v_op(f); _vs[0] += 1
        Lsl = Xs[(GK - 1) % 2]
        def f(v):
            v.drain()
            v.memset(GI[:, 0:1], 0.0).then_inc(vS, 1)
        v_op(f); _vs[0] += 1
        for g in range(1, NG):
            def f(t, g=g, w=_vs[0]):
                t.wait_ge(vS, w)
                t.matmul(P[1][0:N, 0:1], P512T[:], GI[:, g - 1:g],
                         start=True, stop=True).then_inc(mmS, 1)
            t_op(f); _mm[0] += 1
            def f(v, g=g, w=_mm[0]):
                v.wait_ge(mmS, w)
                v.tensor_tensor(GI[:, g:g + 1], P[1][0:N, 0:1],
                                Lsl[:, g - 1:g], OP.add).then_inc(vS, 1)
            v_op(f); _vs[0] += 1
        def f(v):
            v.drain()
            v.memset(INIT[:, 0:1], 0.0).then_inc(vS, 1)
        v_op(f); _vs[0] += 1
        for k in range(GK):
            def f(t, k=k, w=_vs[0]):
                t.wait_ge(vS, w)
                rhs = GI if k == 0 else Xs[(k - 1) % 2]
                t.matmul(P[0][0:N, 0:NG], p64T[:], rhs,
                         start=True, stop=True).then_inc(mmS, 1)
            t_op(f); _mm[0] += 1
            def f(v, k=k, w=_mm[0]):
                v.wait_ge(mmS, w)
                v.tensor_tensor(Xs[k % 2], P[0][0:N, 0:NG],
                                F_r[:, k, :], OP.add)
                v.drain()
                if k < GK - 1:
                    v.tensor_copy(I_r[:, k + 1, :], Xs[k % 2]).then_inc(vS, 1)
                else:
                    v.tensor_copy(I_r[:, 0, 1:NG], Xs[k % 2][:, 0:NG - 1]).then_inc(vS, 1)
            v_op(f); _vs[0] += 1
        def mk_ffn1_unit(j):
            def f(t, j=j):
                if j == 0:
                    t.wait_ge(sTR, 16)
                    t.wait_ge(sW1, 16 * DK)
                if j - 1 > 0:
                    t.wait_ge(geSC, j - 1)
                for dk in range(DK):
                    mm = t.matmul(P[2 + j % 2][0:128, 0:ST],
                                  w1b_r[:, dk, j * 128:(j + 1) * 128],
                                  xfm_r[:, 0, dk, :],
                                  start=(dk == 0), stop=(dk == DK - 1))
                mm.then_inc(mmH, 1)
            return f

        n_pre = len(sched_t)
        sched_t2 = []
        ins_pos = {max(1, (k + 1) * n_pre // HJ): k for k in range(HJ)}
        for idx, fn in enumerate(sched_t):
            sched_t2.append(fn)
            if idx + 1 in ins_pos:
                sched_t2.append(mk_ffn1_unit(ins_pos[idx + 1]))
        sched_t = sched_t2

        pb_cnt = []
        for k in range(T // 2):
            def f(t, k=k, w=_vs[0]):
                t.wait_ge(vS, w)
                src = INIT[:, :] if k == 0 else Sst_r[:, (k - 1) % 2, :]
                t.matmul(P[1][0:N, 0:CC], abTB[:], src,
                         start=True, stop=True).then_inc(mmS, 1)
                t.matmul(P[0][0:N, 0:CC], ab2Tb[:], src,
                         start=True, stop=False)
                t.matmul(P[0][0:N, 0:CC], abTB[:], bu_r[:, 2 * k, :],
                         start=False, stop=True).then_inc(mmS, 1)
            t_op(f); _mm[0] += 2
            def f(v, k=k, w=_mm[0]):
                v.wait_ge(mmS, w)
                if k >= 1:
                    v.wait_ge(xsSC, 2 * k - 1)
                v.tensor_tensor(INIT[:, :], P[1][0:N, 0:CC],
                                bu_r[:, 2 * k, :], OP.add).then_inc(vS, 1)
                if k >= 2:
                    v.wait_ge(xsSC, 2 * k - 2)
                v.tensor_tensor(Sst_r[:, k % 2, :], P[0][0:N, 0:CC],
                                bu_r[:, 2 * k + 1, :], OP.add).then_inc(vS, 1)
            v_op(f)
            _vs[0] += 2
            pb_cnt.append((_vs[0] - 1, _vs[0]))

        with nc.Block() as blk:
            @blk.tensor
            def _(t):
                for fn in sched_t:
                    fn(t)

            @blk.vector
            def _(v):
                for fn in sched_v:
                    fn(v)

            @blk.gpsimd
            def _(g):
                g.wait_ge(sW2X, 16 * 8)
                for j0 in range(0, 16, 4):
                    pdma(g, w2q[(j0 // 4) % 4], w2b_r[:, j0:j0 + 4, :],
                         w2s_r[:, j0:j0 + 4, 0:512])

            @blk.scalar
            def _(s_):
                s_.wait_ge(sW, NSMALL)
                for j in range(HJ):
                    s_.wait_ge(mmH, j + 1)
                    s_.activation(hb_r[:, j, :], P[2 + j % 2][0:128, 0:ST], AF.Gelu,
                                  bias=b1tb[:, j:j + 1]).then_inc(geSC, 1)
                for k in range(T // 2):
                    s_.wait_ge(vS, pb_cnt[k][0])
                    s_.copy(bu_r[:, 2 * k, :], INIT[:, :]).then_inc(xsSC, 1)
                    s_.wait_ge(vS, pb_cnt[k][1])
                    s_.copy(bu_r[:, 2 * k + 1, :],
                            Sst_r[:, k % 2, :]).then_inc(xsSC, 1)

        TRBASE = 16 * DK * 4 * NS

        with nc.Block() as blk:
            @blk.sync
            def _(sy):
                for s in range(1, NS):
                    if s >= 2:
                        sy.wait_ge(vXT, 4 * (s - 1))
                    pdma(sy, (sX2, sX22)[s % 2], xt_r[:, s % 2, :, :],
                         x_r[:, s, :, :])
                    if s == 1:
                        sy.wait_ge(sSCR, 16 * ((NS + 1) // 2))
                        sy.wait_ge(sSCR2, 16 * (NS // 2))
                    if s >= 2:
                        sy.wait_ge(mmH, HJ * (s - 1))
                    pdma(sy, (sTR, sTR2)[s % 2], xfm_r[:, s % 2, :, :],
                         xscr2[s * 128:(s + 1) * 128, :])

            @blk.gpsimd
            def _(g):
                gidx = 0
                for s in range(NS):
                    for dh in range(2):
                        for j0 in range(0, HJ, 4):
                            if gidx < 16:
                                gidx += 4
                                continue
                            g.wait_ge(w2c, gidx - 12)
                            sl = gidx % 16
                            pdma(g, w2q[(gidx // 4) % 4], w2b_r[:, sl:sl + 4, :],
                                 w2s_r[:, j0:j0 + 4, dh * 512:(dh + 1) * 512])
                            gidx += 4

            TPOS = {2: 0, 10: 1, 18: 2, 26: 3}

            @blk.tensor
            def _(t):
                t.wait_ge(sW, NSMALL)
                for s in range(NS):
                    if s > 0:
                        t.wait_ge((sTR, sTR2)[s % 2], 16 * (s // 2 + 1))
                        for j in range(HJ):
                            gate = HJ * s + j - 1
                            if gate > 0:
                                t.wait_ge(geSC, gate)
                            for dk in range(DK):
                                mm = t.matmul(P[j % 2][0:128, 0:ST],
                                              w1b_r[:, dk, j * 128:(j + 1) * 128],
                                              xfm_r[:, s % 2, dk, :],
                                              start=(dk == 0), stop=(dk == DK - 1))
                            mm.then_inc(mmH, 1)
                    t.wait_ge(geSC, HJ * (s + 1))
                    for dh in range(2):
                        G0 = s * 8 + dh * 4
                        banks = [2 + ((G0 + q) % 4) for q in range(4)]
                        for j in range(HJ):
                            gch = (s * 2 + dh) * HJ + j
                            bidx = gch // 4
                            if gch % 4 == 0:
                                t.wait_ge(w2q[bidx % 4], 16 * (bidx // 4 + 1))
                            sl = gch % 16
                            for q in range(4):
                                if j == 0 and G0 + q >= 4:
                                    t.wait_ge(vEV, G0 + q - 4 + 1)
                                mm = t.matmul(P[banks[q]][0:128, 0:512],
                                              hb_r[:, j, q * 128:(q + 1) * 128],
                                              w2b_r[:, sl, :], start=(j == 0),
                                              stop=False)
                            mm.then_inc(w2c, 1)
                        for q in range(4):
                            t.matmul(P[banks[q]][0:128, 0:512],
                                     bu[0:N + 1, (4 * s + q) * TT:(4 * s + q + 1) * TT],
                                     ctb[0:N + 1, dh * 512:(dh + 1) * 512],
                                     start=False, stop=True).then_inc(mmF, 1)

            @blk.scalar
            def _(s_):
                for s in range(NS):
                    if s > 0:
                        for j in range(HJ):
                            s_.wait_ge(mmH, HJ * s + j + 1)
                            s_.activation(hb_r[:, j, :], P[j % 2][0:128, 0:ST],
                                          AF.Gelu,
                                          bias=b1tb[:, j:j + 1]).then_inc(geSC, 1)
                    for q in range(4):
                        s_.wait_ge(vEV, 8 * s + 4 + q + 1)
                        pdma(s_, sOUT, o[(4 * s + q) * TT:(4 * s + q + 1) * TT, :],
                             outb_r[:, q, :])
                s_.wait_ge(sOUT, 16 * 4 * NS)

            @blk.vector
            def _(v):
                v.wait_ge(sW, NSMALL)
                for s in range(NS):
                    v.wait_ge((sX2, sX22)[s % 2], 16 * (s // 2 + 1))
                    for q in range(4):
                        if s >= 1 and q == 0:
                            v.wait_ge(sOUT, 16 * 4 * s)
                        col = 4 * s + q
                        v.tensor_scalar(outb_r[:, q, :], xt_r[:, s % 2, q, :],
                                        rstd[:, col:col + 1], mrn[:, col:col + 1],
                                        OP.mult, OP.add)
                        v.drain()
                        v.tensor_tensor(outb_r[:, q, :], outb_r[:, q, :], dsb[:],
                                        OP.mult)
                        v.drain()
                        v.tensor_tensor(outb_r[:, q, :], outb_r[:, q, :],
                                        xt_r[:, s % 2, q, :], OP.add).then_inc(vXT, 1)
                        v.drain()
                    for dh in range(2):
                        G0 = s * 8 + dh * 4
                        banks = [2 + ((G0 + q) % 4) for q in range(4)]
                        for q in range(4):
                            v.wait_ge(mmF, G0 + q + 1)
                            v.tensor_tensor(outb_r[:, q, dh * 512:(dh + 1) * 512],
                                            outb_r[:, q, dh * 512:(dh + 1) * 512],
                                            P[banks[q]][0:128, 0:512],
                                            OP.add).then_inc(vEV, 1)

        if debug_taps:
            with nc.Block() as blk:
                dbg_sem = sem("dbg_sem")
                outs = []
                for name in debug_taps:
                    h = tens[name]
                    outs.append((nc.dram_tensor(f"dbg_{name}", list(h.shape),
                                                h.dtype, kind="ExternalOutput").ap(), h))

                @blk.gpsimd
                def _(g):
                    n = 0
                    for dst, h in outs:
                        pdma(g, dbg_sem, dst[:, :], h[:])
                        n += 16
                    g.wait_ge(dbg_sem, n)

    return nc


def _host_inputs(ins):
    bf = ml_dtypes.bfloat16
    A = np.ascontiguousarray(ins["L0_attn_A"], dtype=np.float32)
    B = np.ascontiguousarray(ins["L0_attn_B"], dtype=np.float32)
    C = np.ascontiguousarray(ins["L0_attn_C"], dtype=np.float32)
    Dv = np.ascontiguousarray(ins["L0_attn_D"], dtype=np.float32)
    logdt = np.ascontiguousarray(ins["L0_attn_log_dt"], dtype=np.float32)
    w1 = np.ascontiguousarray(ins["L0_ffn_w1"], dtype=np.float32)
    b1 = np.ascontiguousarray(ins["L0_ffn_b1"], dtype=np.float32)
    w2 = np.ascontiguousarray(ins["L0_ffn_w2"], dtype=np.float32)
    b2 = np.ascontiguousarray(ins["L0_ffn_b2"], dtype=np.float32)
    return dict(
        w1=w1,
        w2=w2,
        bT=np.ascontiguousarray(np.pad(B.T, ((0, 0), (0, 128 - N)))),
        cT=np.ascontiguousarray(np.vstack([C.T, b2.reshape(1, D)])),
        aN=A,
        aT=np.ascontiguousarray(A.T),
        i64=np.eye(N, dtype=np.float32),
        d_rep=np.ascontiguousarray(np.broadcast_to(Dv, (128, D))),
        ldr=np.ascontiguousarray(np.broadcast_to(logdt, (128, D))),
        b1t=np.ascontiguousarray(b1.reshape(HJ, 128).T),
        eye128=np.eye(128, dtype=bf),
        ones8k=np.ones((1, ins["L0_attn_u"].shape[1]), dtype=bf),
        cst=np.ascontiguousarray(np.broadcast_to(
            np.array([1.0 / D, -1.0 / D, 1e-5, -1.0], dtype=np.float32), (128, 4))),
    )


def kernel(L0_attn_u, L0_attn_A, L0_attn_B, L0_attn_C, L0_attn_D, L0_attn_log_dt,
           L0_ffn_w1, L0_ffn_b1, L0_ffn_w2, L0_ffn_b2):
    ins = dict(L0_attn_u=L0_attn_u, L0_attn_A=L0_attn_A, L0_attn_B=L0_attn_B,
               L0_attn_C=L0_attn_C, L0_attn_D=L0_attn_D,
               L0_attn_log_dt=L0_attn_log_dt, L0_ffn_w1=L0_ffn_w1,
               L0_ffn_b1=L0_ffn_b1, L0_ffn_w2=L0_ffn_w2, L0_ffn_b2=L0_ffn_b2)
    u = np.ascontiguousarray(np.asarray(L0_attn_u, dtype=np.float32))
    Bn, L, _ = u.shape
    shared = _host_inputs(ins)
    nc = build_nc(L)
    in_maps = [dict(x=np.ascontiguousarray(u[i]), **shared) for i in range(Bn)]
    res = run_bass_kernel_spmd(nc, in_maps, core_ids=list(range(Bn)))
    return np.stack([r["o"] for r in res.results], axis=0)
